# version 09
# baseline (speedup 1.0000x reference)
import numpy as np
import ml_dtypes
import concourse.bass as bass
import concourse.mybir as mybir
from concourse.bass_utils import run_bass_kernel_spmd

F32 = mybir.dt.float32
F32R = mybir.dt.float32r
BF16 = mybir.dt.bfloat16
AF = mybir.ActivationFunctionType
ALU = mybir.AluOpType
AX = mybir.AxisListType

D_MODEL = 2048
D_INNER = 4096
DSTATE = 16
SEQ = 4096
TC = 512
NCHUNK = SEQ // TC
NCT = 32
EPS_RMS = 1e-5
EPS_LN = 1e-5
EPS_F32 = float(np.finfo(np.float32).eps)
MAGIC = 12582912.0

KT_D = 16
MT_PROJ = 81
MT_OP = 16
KT_OP = 32
MT_G = 64
KT_DN = 64
MT_GU2 = 128
KT_P = 64
MT_Z0, MT_XB0, MT_B0, MT_C0, MT_DTR = 0, 32, 40, 48, 80
NWSL = 4

DEBUG_CHUNK0 = []
BUILD_CHUNKS = NCHUNK
BUILD_PAIRS = 8


class Sched:
    ENGINES = ("sync", "gpsimd", "vector", "scalar", "tensor")

    def __init__(self):
        self.cnt = {}
        self.ops = {e: [] for e in self.ENGINES}
        self.last_wait = {e: {} for e in self.ENGINES}

    def c(self, sem):
        return self.cnt.get(sem, 0)

    def emit(self, eng, fn, waits=(), inc=None, inc_n=1):
        w = []
        for sem, val in waits:
            if val <= 0 or self.last_wait[eng].get(sem, -1) >= val:
                continue
            self.last_wait[eng][sem] = val
            w.append((sem, val))
        if inc is not None:
            self.cnt[inc] = self.c(inc) + inc_n
        self.ops[eng].append((fn, w, inc, inc_n))

    def run(self, eng_name, eng, sems):
        for fn, waits, inc, inc_n in self.ops[eng_name]:
            for sem, val in waits:
                eng.wait_ge(sems[sem], val)
            inst = fn(eng)
            if inc is not None:
                inst.then_inc(sems[inc], inc_n)


def build_nc(n_chunks=None, debug_chunk0=None, sync_barriers=()):
    n_chunks = BUILD_CHUNKS if n_chunks is None else n_chunks
    debug_chunk0 = DEBUG_CHUNK0 if debug_chunk0 is None else debug_chunk0
    nc = bass.Bass("TRN2", target_bir_lowering=False)
    S = Sched()

    d_x = nc.dram_tensor("x", [SEQ, D_MODEL], F32, kind="ExternalInput")
    d_ipw = nc.dram_tensor("ipw", [MT_PROJ, 128, KT_D * 128], F32, kind="ExternalInput")
    d_dtw = nc.dram_tensor("dtw", [32, 128, 128], F32, kind="ExternalInput")
    d_opw = nc.dram_tensor("opw", [MT_OP, 128, KT_OP * 128], F32, kind="ExternalInput")
    d_gw = nc.dram_tensor("gw", [MT_G, 128, KT_D * 128], F32, kind="ExternalInput")
    d_uw = nc.dram_tensor("uw", [MT_G, 128, KT_D * 128], F32, kind="ExternalInput")
    d_dw = nc.dram_tensor("dw", [MT_OP, 128, KT_DN * 128], F32, kind="ExternalInput")
    d_qgu = nc.dram_tensor("qgu", [2 * 8192, D_MODEL], F32, kind="ExternalInput")
    d_qp = nc.dram_tensor("qp", [D_MODEL, 8192], F32, kind="ExternalInput")
    d_convw = nc.dram_tensor("convw", [D_INNER, 4], F32, kind="ExternalInput")
    d_convb = nc.dram_tensor("convb", [D_INNER], F32, kind="ExternalInput")
    d_dtb = nc.dram_tensor("dtb", [D_INNER], F32, kind="ExternalInput")
    d_alog = nc.dram_tensor("alog", [D_INNER, DSTATE], F32, kind="ExternalInput")
    d_dpar = nc.dram_tensor("dpar", [D_INNER], F32, kind="ExternalInput")
    d_ln1 = nc.dram_tensor("ln1", [D_MODEL], F32, kind="ExternalInput")
    d_ln2 = nc.dram_tensor("ln2", [D_MODEL], F32, kind="ExternalInput")
    d_RB = nc.dram_tensor("RB", [32, 64, 128], F32, kind="ExternalInput")
    d_RC = nc.dram_tensor("RC", [16, 128, 128], F32, kind="ExternalInput")
    d_R4 = nc.dram_tensor("R4", [2, 64, 128], F32, kind="ExternalInput")
    d_I = nc.dram_tensor("Imat", [128, 128], F32, kind="ExternalInput")
    d_Ib = nc.dram_tensor("Imatb", [128, 128], BF16, kind="ExternalInput")
    d_onesb = nc.dram_tensor("onesb", [128, 1], BF16, kind="ExternalInput")
    d_fconst = nc.dram_tensor("fconst", [128, 8], F32, kind="ExternalInput")
    d_ones1 = nc.dram_tensor("ones1", [1, 128], F32, kind="ExternalInput")
    d_out = nc.dram_tensor("out", [SEQ, D_MODEL], F32, kind="ExternalOutput")

    s_ipw = nc.dram_tensor("s_ipw", [MT_PROJ, 128, KT_D * 128], BF16, kind="Internal")
    s_dtw = nc.dram_tensor("s_dtw", [32, 128, 128], BF16, kind="Internal")
    s_opw = nc.dram_tensor("s_opw", [MT_OP, 128, KT_OP * 128], BF16, kind="Internal")
    s_gw = nc.dram_tensor("s_gw", [MT_G, 128, KT_D * 128], BF16, kind="Internal")
    s_uw = nc.dram_tensor("s_uw", [MT_G, 128, KT_D * 128], BF16, kind="Internal")
    s_dw = nc.dram_tensor("s_dw", [MT_OP, 128, KT_DN * 128], BF16, kind="Internal")
    s_qgu = nc.dram_tensor("s_qgu", [MT_GU2, 128, KT_D * 128], BF16, kind="Internal")
    s_qp = nc.dram_tensor("s_qp", [MT_OP, 128, KT_P * 128], BF16, kind="Internal")

    from contextlib import ExitStack
    stack = ExitStack()
    E = stack.enter_context

    RB_sb = E(nc.sbuf_tensor("RB_sb", [128, 32 * 128], BF16))
    RC_sb = E(nc.sbuf_tensor("RC_sb", [128, 16 * 128], BF16))
    R4_sb = E(nc.sbuf_tensor("R4_sb", [128, 2 * 128], BF16))
    If_sb = E(nc.sbuf_tensor("If_sb", [128, 128], F32))
    Ib_sb = E(nc.sbuf_tensor("Ib_sb", [128, 128], BF16))
    onesb_sb = E(nc.sbuf_tensor("onesb_sb", [128, 1], BF16))
    fconst_sb = E(nc.sbuf_tensor("fconst_sb", [128, 8], F32))
    ones1_sb = E(nc.sbuf_tensor("ones1_sb", [1, 128], F32R))
    A_sb = E(nc.sbuf_tensor("A_sb", [128, NCT * DSTATE], F32))
    convw_sb = E(nc.sbuf_tensor("convw_sb", [128, NCT * 4], F32))
    convb_sb = E(nc.sbuf_tensor("convb_sb", [128, NCT], F32))
    dtb_sb = E(nc.sbuf_tensor("dtb_sb", [128, NCT], F32))
    dpar_sb = E(nc.sbuf_tensor("dpar_sb", [128, NCT], F32))
    ln1_sb = E(nc.sbuf_tensor("ln1_sb", [128, KT_D], F32))
    ln2_sb = E(nc.sbuf_tensor("ln2_sb", [128, KT_D], F32))

    poolF1 = E(nc.sbuf_tensor("poolF1", [128, 8192], F32))
    poolF2 = E(nc.sbuf_tensor("poolF2", [128, 8192], F32))
    x_fm = E(nc.sbuf_tensor("x_fm", [128, KT_D * TC], F32))
    poolF4 = E(nc.sbuf_tensor("poolF4", [128, 4096], F32))
    poolB1 = E(nc.sbuf_tensor("poolB1", [128, 16384], BF16))
    h_bf = E(nc.sbuf_tensor("h_bf", [128, KT_D * TC], BF16))
    dtr_bf = E(nc.sbuf_tensor("dtr_bf", [128, TC], BF16))
    wpan = E(nc.sbuf_tensor("wpan", [128, NWSL * 2048], BF16))
    carry = E(nc.sbuf_tensor("carry", [128, NCT * DSTATE], F32))
    tails = E(nc.sbuf_tensor("tails", [128, NCT * 3], F32))
    stats = E(nc.sbuf_tensor("stats", [1, 2 * TC], F32))
    statr = E(nc.sbuf_tensor("statr", [1, 2 * TC], F32R))
    statC_r = statr[0:1, TC:2 * TC]
    statC_f = statr[0:1, TC:2 * TC].bitcast(F32)
    yc_t = E(nc.sbuf_tensor("yc_t", [128, 2 * TC], BF16))
    qsc = E(nc.sbuf_tensor("qsc", [128, 288], F32))
    qmb = E(nc.sbuf_tensor("qmb", [128, 96], BF16))

    pb = [E(nc.psum_tensor(f"pb{i}", [128, TC], F32)) for i in range(2)]
    pbB = E(nc.psum_tensor("pbB", [128, 2 * TC], F32))
    pbC = E(nc.psum_tensor("pbC", [128, 2 * TC], F32))
    pst = E(nc.psum_tensor("pst", [128, TC], F32))
    pbc = E(nc.psum_tensor("pbc", [128, TC], F32))
    bank_ap = {"b0": pb[0][:, :], "b1": pb[1][:, :],
               "B0": pbB[:, 0:TC], "B1": pbB[:, TC:2 * TC], "BB": pbB[:, :],
               "C0": pbC[:, 0:TC], "C1": pbC[:, TC:2 * TC], "CC": pbC[:, :],
               "st": pst[:, :], "bc": pbc[:, :]}

    sems = {n: E(nc.semaphore(n)) for n in ["W", "G", "Gc", "P", "V", "Sc", "U", "C"]}
    block = E(nc.Block())

    FCONST_VALS = [EPS_RMS, 1.0, EPS_F32, 1.0 / D_MODEL, -0.5, 1.0 / 8192.0]
    for fi, fv in enumerate(FCONST_VALS):
        nc.const_aps.register(F32, fv, fconst_sb[:, fi:fi + 1])

    zsil = lambda q: poolF1[:, q * TC:(q + 1) * TC]
    dt_sl = lambda q: poolF1[:, 2048 + q * TC: 2048 + (q + 1) * TC]
    cacc = lambda q: poolF1[:, 4096 + q * TC: 4096 + (q + 1) * TC]
    dtx_sl = cacc
    xsil = lambda q: poolF1[:, 6144 + q * TC: 6144 + (q + 1) * TC]
    hres = lambda m: poolF1[:, m * TC:(m + 1) * TC]
    sp_a = lambda q: poolF2[:, q * TC:(q + 1) * TC]
    xcb = lambda ci: poolF2[:, 3072 + ci * 520: 3072 + ci * 520 + 515]
    dA_b = lambda i: poolF2[:, 1024 + i * TC: 1024 + (i + 1) * TC]
    dbx2 = poolF2[:, 5152:6176]
    hs2 = poolF2[:, 6176:7200]
    yc2 = yc_t[:, 0:2 * TC]
    ygf = poolF2[:, 7232:7744]
    t1pre = lambda m: poolF2[:, m * TC:(m + 1) * TC]
    xin_pt = lambda h: poolF4[:, h * D_MODEL:(h + 1) * D_MODEL]
    C_sl = lambda q: poolF4[:, q * 256:(q + 1) * 256].bitcast(BF16)
    B_half = lambda h: poolF4[h * 64:h * 64 + 64, 1024:1280].bitcast(BF16)
    B_dst = poolF4[:, 1024:1280].bitcast(BF16)
    xb_half = lambda h: poolF4[h * 64:h * 64 + 64, 1280:1536].bitcast(BF16)
    xb_dst = poolF4[:, 1280:1536].bitcast(BF16)
    outst = lambda i: poolF4[:, i * D_MODEL:(i + 1) * D_MODEL]
    ygt = lambda ct: poolB1[:, ct * TC:(ct + 1) * TC]
    hring = lambda i: poolB1[:, i * TC:(i + 1) * TC]
    gs_b = lambda i: poolB1[:, 8192 + i * TC: 8192 + (i + 1) * TC]
    sq_b = lambda i: poolB1[:, 9216 + i * TC: 9216 + (i + 1) * TC]
    QB_COLS = [0, 2048, 10240]
    WQ_COLS = [4096, 6144, 12288]
    qbf = lambda i: poolB1[:, QB_COLS[i]:QB_COLS[i] + 2048]
    wq_s = lambda i: poolB1[:, WQ_COLS[i]:WQ_COLS[i] + 2048]
    qstage = lambda i: poolB1[:, 8192 + i * TC: 8192 + (i + 1) * TC]
    qab = lambda i: x_fm[:, i * 2048:(i + 1) * 2048]
    wslot = lambda s: wpan[:, s * 2048:(s + 1) * 2048]

    d_dbg = {}
    dbg_specs = {
        "h_bf": (lambda: h_bf[:, :], BF16, KT_D * TC),
        "dt": (lambda: poolF1[:, 2048:4096], F32, 2048),
        "xsil": (lambda: poolF1[:, 6144:8192], F32, 2048),
        "zsil": (lambda: poolF1[:, 0:2048], F32, 2048),
        "ygated": (lambda: poolB1[:, 0:NCT * TC], BF16, NCT * TC),
        "hres": (lambda: poolF1[:, :], F32, 8192),
        "t1pre": (lambda: poolF2[:, :], F32, 8192),
        "x1bf": (lambda: h_bf[:, :], BF16, KT_D * TC),
        "carry": (lambda: carry[:, :], F32, NCT * DSTATE),
        "xfm": (lambda: x_fm[:, :], F32, KT_D * TC),
        "x1f": (lambda: poolF2[:, :], F32, 8192),
        "t2pre": (lambda: poolF1[:, :], F32, 8192),
        "t2": (lambda: poolF1[:, :], F32, 8192),
        "hbf2": (lambda: h_bf[:, :], BF16, KT_D * TC),
        "lnstats": (lambda: stats[0:1, :], F32, 2 * TC, 1),
        "lnstatr": (lambda: statr[0:1, :].bitcast(F32), F32, 2 * TC, 1),
    }
    for name in debug_chunk0:
        if name not in dbg_specs:
            continue
        spec = dbg_specs[name]
        ap_fn, dt_, ncol = spec[0], spec[1], spec[2]
        nrow = spec[3] if len(spec) > 3 else 128
        d_dbg[name] = nc.dram_tensor("dbg_" + name, [nrow, ncol], dt_, kind="ExternalOutput")

    def dma(q, dst, src, sem="W", waits=(), **kw):
        S.emit(q, lambda e, dst=dst, src=src, kw=kw: e.dma_start(dst, src, **kw),
               waits=waits, inc=sem, inc_n=16)

    def t_mm(out, lhsT, rhs, start, stop, waits=(), inc=None):
        S.emit("tensor",
               lambda e, o=out, l=lhsT, r=rhs, a=start, b=stop: e.matmul(o, l, r, start=a, stop=b),
               waits=waits, inc=inc)

    def t_tr(out, in_, ident, waits=(), inc=None):
        S.emit("tensor", lambda e, o=out, i=in_, d=ident: e.transpose(o, i, d),
               waits=waits, inc=inc)

    def v_op(fn, waits=()):
        S.emit("vector", fn, waits=waits, inc="V")

    def s_op(fn, waits=()):
        S.emit("scalar", fn, waits=waits, inc="Sc")

    def sbar(site):
        if site in sync_barriers:
            S.emit("sync", lambda e: e.nop(),
                   waits=[("V", S.c("V")), ("Sc", S.c("Sc")), ("P", S.c("P"))])

    def dbg_dump(name):
        if name in d_dbg:
            ap_fn = dbg_specs[name][0]
            dma("sync", d_dbg[name][:, :], ap_fn(), "W",
                waits=[("V", S.c("V")), ("Sc", S.c("Sc")), ("P", S.c("P"))])

    def bcast_ap(handle, rowpitch, c0, ngroups, reps):
        return bass.AP(handle, c0, [[rowpitch, 128], [1, ngroups], [0, reps]])

    for v in range(32):
        for hh in range(2):
            dma("gpsimd", RB_sb[hh * 64:(hh + 1) * 64, v * 128:(v + 1) * 128],
                d_RB[v, :, :], "G")
    for v in range(16):
        dma("gpsimd", RC_sb[:, v * 128:(v + 1) * 128], d_RC[v, :, :], "G")
    for v in range(2):
        for hh in range(2):
            dma("gpsimd", R4_sb[hh * 64:(hh + 1) * 64, v * 128:(v + 1) * 128],
                d_R4[v, :, :], "G")
    dma("gpsimd", If_sb[:, :], d_I[:, :], "G")
    dma("gpsimd", Ib_sb[:, :], d_Ib[:, :], "G")
    dma("gpsimd", onesb_sb[:, :], d_onesb[:, :], "G")
    dma("gpsimd", fconst_sb[:, :], d_fconst[:, :], "G")
    dma("gpsimd", ones1_sb[:, :], d_ones1[:, :].bitcast(F32R), "G")
    for ct in range(NCT):
        dma("gpsimd", A_sb[:, ct * 16:(ct + 1) * 16], d_alog[ct * 128:(ct + 1) * 128, :], "G")
        dma("gpsimd", convw_sb[:, ct * 4:(ct + 1) * 4], d_convw[ct * 128:(ct + 1) * 128, :], "G")
        dma("gpsimd", convb_sb[:, ct:ct + 1], d_convb[ct * 128:(ct + 1) * 128], "G")
        dma("gpsimd", dtb_sb[:, ct:ct + 1], d_dtb[ct * 128:(ct + 1) * 128], "G")
        dma("gpsimd", dpar_sb[:, ct:ct + 1], d_dpar[ct * 128:(ct + 1) * 128], "G")
    for k in range(KT_D):
        dma("gpsimd", ln1_sb[:, k:k + 1], d_ln1[k * 128:(k + 1) * 128], "G")
        dma("gpsimd", ln2_sb[:, k:k + 1], d_ln2[k * 128:(k + 1) * 128], "G")
    g_consts = S.c("G")

    s_op(lambda e: e.activation(A_sb[:, :], A_sb[:, :], AF.Exp), waits=[("G", g_consts)])
    v_op(lambda e: e.tensor_scalar_mul(A_sb[:, :], A_sb[:, :], -1.0), waits=[("Sc", S.c("Sc"))])
    v_op(lambda e: e.memset(carry[:, :], 0.0))
    v_op(lambda e: e.memset(tails[:, :], 0.0))

    pbq = [pbB[:, 0:TC].bitcast(BF16), pbC[:, 0:TC].bitcast(BF16)]
    qslot_free = [("P", 0), ("P", 0), ("P", 0)]
    pbq_free = [("Sc", 0), ("Sc", 0)]
    qst_free = [("W", 0)] * 4
    qst_free = list(qst_free)

    wqs_free = [("P", 0), ("P", 0), ("P", 0)]
    qab_free = [("Gc", 0), ("Gc", 0), ("Gc", 0)]

    def emit_quant(d_src, scr, sub_panels):
        for i, (r0, c0, m_, k0) in enumerate(sub_panels):
            sl = i % 3
            dma("gpsimd", qbf(sl), d_src[r0:r0 + 128, c0:c0 + 2048], "G",
                waits=[qslot_free[sl]], max_dma_last_dim=8192)
            gq = S.c("G")
            s_op(lambda e, sl=sl: e.activation(qab(sl), qbf(sl), AF.Abs),
                 waits=[("G", gq), qab_free[sl]])
            scv = S.c("Sc")
            qs = qsc[:, sl * 32:sl * 32 + 32]
            qm = qsc[:, 96 + sl * 32: 128 + sl * 32]
            qi = qsc[:, 192 + sl * 32: 224 + sl * 32]
            qmbv = qmb[:, sl * 32:(sl + 1) * 32]
            v_op(lambda e, sl=sl, qs=qs: e.tensor_reduce(
                qs, qab(sl).rearrange("p (g s) -> p g s", s=64), axis=AX.X, op=ALU.add),
                waits=[("Sc", scv)])
            v_op(lambda e, qs=qs, qmbv=qmbv: e.tensor_scalar_mul(qmbv, qs, 1.0 / 64),
                 waits=[("V", S.c("V"))])
            v_op(lambda e, qm=qm, qmbv=qmbv: e.tensor_scalar_max(qm, qmbv, 1e-8),
                 waits=[("V", S.c("V")), qab_free[sl]])
            v_op(lambda e, qi=qi, qm=qm: e.reciprocal(qi, qm),
                 waits=[("V", S.c("V"))])
            v_rec = S.c("V")
            qi_bc = bcast_ap(qsc, 288, 192 + sl * 32, 32, 64)
            qm_bc = bcast_ap(qsc, 288, 96 + sl * 32, 32, 64)
            S.emit("gpsimd", lambda e, sl=sl, qi_bc=qi_bc: e.tensor_tensor(
                wq_s(sl).rearrange("p (g s) -> p g s", s=64),
                qbf(sl).rearrange("p (g s) -> p g s", s=64), qi_bc, op=ALU.mult),
                waits=[("G", gq), ("V", v_rec), wqs_free[sl]], inc="Gc")
            gdiv = S.c("Gc")
            qslot_free[sl] = ("Gc", gdiv)
            v_op(lambda e, sl=sl: e.tensor_scalar(
                qab(sl), wq_s(sl), MAGIC, MAGIC, op0=ALU.add, op1=ALU.subtract),
                waits=[("Gc", gdiv)])
            v_op(lambda e, sl=sl: e.tensor_scalar(
                qab(sl), qab(sl), -1.0, 1.0, op0=ALU.max, op1=ALU.min))
            v_rnd = S.c("V")
            S.emit("gpsimd", lambda e, sl=sl, qm_bc=qm_bc: e.tensor_tensor(
                wq_s(sl).rearrange("p (g s) -> p g s", s=64),
                qab(sl).rearrange("p (g s) -> p g s", s=64), qm_bc, op=ALU.mult),
                waits=[("V", v_rnd)], inc="Gc")
            gscale = S.c("Gc")
            qab_free[sl] = ("Gc", gscale)
            for g in range(2):
                pq = g
                for j in range(8):
                    jj = g * 8 + j
                    t_tr(pbq[pq][:, j * 128:(j + 1) * 128],
                         wq_s(sl)[:, jj * 128:(jj + 1) * 128], Ib_sb[:, :],
                         waits=[("Gc", gscale), pbq_free[pq], ("G", g_consts)],
                         inc="P" if j == 7 else None)
                pq_P = S.c("P")
                for h in range(2):
                    stn = (i * 4 + g * 2 + h) % 4
                    st_ap = qstage(stn)
                    s_op(lambda e, pq=pq, h=h, st_ap=st_ap: e.copy(
                        st_ap, pbq[pq][:, h * 512:(h + 1) * 512]),
                        waits=[("P", pq_P), qst_free[stn]])
                    if h == 1:
                        pbq_free[pq] = ("Sc", S.c("Sc"))
                    scx = S.c("Sc")
                    kb = (k0 + g * 8 + h * 4) * 128
                    dma("sync", scr[m_, :, kb:kb + 512], st_ap, "W", waits=[("Sc", scx)])
                    qst_free[stn] = ("W", S.c("W"))
            wqs_free[sl] = ("P", S.c("P"))

    qgu_panels = [(mp * 128, 0, mp, 0) for mp in range(MT_GU2)]
    emit_quant(d_qgu, s_qgu, qgu_panels)
    qp_panels = [(mp * 128, cs * 2048, mp, cs * 16)
                 for mp in range(MT_OP) for cs in range(4)]
    emit_quant(d_qp, s_qp, qp_panels)
    quant_done_W = S.c("W")
    quant_done_V = S.c("V")
    quant_done_Sc = S.c("Sc")
    if "qgu0" in debug_chunk0:
        d_dbg["qgu0"] = nc.dram_tensor("dbg_qgu0", [128, 2048], BF16, kind="ExternalOutput")
        dma("sync", d_dbg["qgu0"][:, :], s_qgu[0, :, :], "W", waits=[("W", quant_done_W)])
        d_dbg["qp0"] = nc.dram_tensor("dbg_qp0", [128, 2048], BF16, kind="ExternalOutput")
        dma("sync", d_dbg["qp0"][:, :], s_qp[0, :, 0:2048], "W", waits=[("W", quant_done_W)])

    wready = {}
    for dsrc, dscr, ntiles in [(d_ipw, s_ipw, MT_PROJ * KT_D), (d_dtw, s_dtw, 32),
                               (d_opw, s_opw, MT_OP * KT_OP), (d_gw, s_gw, MT_G * KT_D),
                               (d_uw, s_uw, MT_G * KT_D), (d_dw, s_dw, MT_OP * KT_DN)]:
        fsrc = dsrc.ap().rearrange("a p f -> (a p f)")
        fdst = dscr.ap().rearrange("a p f -> (a p f)")
        nelem = ntiles * 16384
        step = 8 * 1024 * 1024
        for off in range(0, nelem, step):
            ln = min(step, nelem - off)
            dma("gpsimd", fdst[off:off + ln], fsrc[off:off + ln], "G",
                max_dma_last_dim=8192)
        wready[dscr.name] = ("G", S.c("G"))

    bankF = {k: [("V", 0)] for k in
             ["b0", "b1", "B0", "B1", "C0", "C1", "st", "bc"]}
    bankF["B0"] = [("Sc", quant_done_Sc)]
    bankF["C0"] = [("Sc", quant_done_Sc)]

    def wbank(k):
        if k == "BB":
            return bankF["B0"] + bankF["B1"]
        if k == "CC":
            return bankF["C0"] + bankF["C1"]
        return bankF[k]

    def setbank(k, ent):
        if k in ("BB",):
            bankF["B0"] = [ent]; bankF["B1"] = [ent]; return
        if k in ("CC",):
            bankF["C0"] = [ent]; bankF["C1"] = [ent]; return
        bankF[k] = [ent]
    wsl_free = [("P", 0)] * NWSL
    wsl_free = list(wsl_free)
    wdma_i = [0]
    x_load_ok = [("W", 0), ("P", 0)]
    outst_free = [("W", 0), ("W", 0)]
    hring_free = [("P", 0)] * 16
    hring_free = list(hring_free)
    gs_free = [("W", quant_done_W), ("W", quant_done_W)]
    sq_free = [("W", quant_done_W), ("W", quant_done_W)]
    dA_free = [("V", 0)] * 4
    dA_free = list(dA_free)
    yc_free = [("P", 0)]

    def wstream(scr, m, ncols, waits_extra=(), col0=0, q="sync", sem="W"):
        s = wdma_i[0] % NWSL
        wdma_i[0] += 1
        dst = wpan[:, s * 2048: s * 2048 + ncols]
        dma(q, dst, scr[m, :, col0:col0 + ncols], sem,
            waits=[wsl_free[s]] + list(waits_extra))
        return s, dst, (sem, S.c(sem))

    def mm_group(bank, lhs_panel, rhs_tiles, waits0=()):
        n = len(rhs_tiles)
        for k, (l_ap, r_ap) in enumerate(rhs_tiles):
            t_mm(bank_ap[bank], l_ap, r_ap, start=(k == 0), stop=(k == n - 1),
                 waits=(list(waits0) + wbank(bank)) if k == 0 else (),
                 inc="P" if k == n - 1 else None)
        return S.c("P")

    def ln_stats_rms(src_tiles, eps, scale_inv, wait_V):
        nt = len(src_tiles)
        for i, ap in enumerate(src_tiles):
            sl = i % 2
            s_op(lambda e, ap=ap, sl=sl: e.activation(sq_b(sl), ap, AF.Square),
                 waits=[sq_free[sl], ("V", wait_V)])
            scv = S.c("Sc")
            t_mm(pst[0:1, :], onesb_sb[:, 0:1], sq_b(sl), start=(i == 0), stop=(i == nt - 1),
                 waits=[("Sc", scv)] + (wbank("st") if i == 0 else []),
                 inc="P")
            sq_free[sl] = ("P", S.c("P"))
        pv = S.c("P")
        s_op(lambda e: e.activation(stats[0:1, 0:512], pst[0:1, :], AF.Ln,
                                    bias=eps, scale=scale_inv), waits=[("P", pv)])
        s_op(lambda e: e.activation(stats[0:1, 512:1024], stats[0:1, 0:512], AF.Exp,
                                    scale=-0.5))
        setbank("st", ("Sc", S.c("Sc")))
        v_op(lambda e: e.tensor_scalar_mul(statr[0:1, 0:512],
                                           stats[0:1, 512:1024], 1.0),
             waits=[("Sc", S.c("Sc"))])
        vv = S.c("V")
        t_mm(pbc[:, :], ones1_sb[:, :], statr[0:1, 0:512], True, True,
             waits=[("V", vv)] + wbank("bc"), inc="P")
        return S.c("P")

    total_pairs = BUILD_PAIRS

    for kc in range(n_chunks):
        for half in range(2):
            for hp in range(2):
                tp = half * 2 + hp
                dma("sync", xin_pt(hp),
                    d_x[kc * TC + tp * 128: kc * TC + (tp + 1) * 128, :],
                    "W", waits=[x_load_ok[0], x_load_ok[1],
                                ("G", g_consts if (kc == 0 and tp == 0) else 0)])
            gx = S.c("W")
            for dg in range(8):
                bk = dg % 2
                bkk = "b%d" % bk
                for j in range(2):
                    dt16 = dg * 2 + j
                    for hp in range(2):
                        t_tr(pb[bk][:, (j * 2 + hp) * 128:(j * 2 + hp + 1) * 128],
                             xin_pt(hp)[:, dt16 * 128:(dt16 + 1) * 128], If_sb[:, :],
                             waits=([("W", gx)] + wbank(bkk) if (j == 0 and hp == 0) else []),
                             inc="P" if (j == 1 and hp == 1) else None)
                pv = S.c("P")
                for j in range(2):
                    dt16 = dg * 2 + j
                    v_op(lambda e, dt16=dt16, bk=bk, j=j, half=half: e.tensor_copy(
                        x_fm[:, dt16 * TC + half * 256: dt16 * TC + half * 256 + 256],
                        pb[bk][:, j * 256:(j + 1) * 256]),
                        waits=[("P", pv),
                               ("Sc", S.c("Sc") if (j == 0 and dg == 0 and half == 0) else 0)])
                setbank(bkk, ("V", S.c("V")))
            x_load_ok = [x_load_ok[0], ("P", S.c("P"))]
        xfm_V = S.c("V")
        sbar("A")
        pv_bc = ln_stats_rms([x_fm[:, m * TC:(m + 1) * TC] for m in range(KT_D)],
                             EPS_RMS, 1.0 / D_MODEL, xfm_V)
        for m in range(KT_D):
            v_op(lambda e, m=m: e.scalar_tensor_tensor(
                h_bf[:, m * TC:(m + 1) * TC], x_fm[:, m * TC:(m + 1) * TC],
                ln1_sb[:, m:m + 1], pbc[:, :], op0=ALU.mult, op1=ALU.mult),
                waits=[("P", pv_bc)])
        setbank("bc", ("V", S.c("V")))
        hbf_V = S.c("V")
        if kc == 0:
            dbg_dump("xfm")
            dbg_dump("h_bf")

        ipw_ready = wready["s_ipw"]
        dtw_ready = wready["s_dtw"]
        s, pan, wt = wstream(s_ipw, MT_DTR, 2048, waits_extra=[ipw_ready])
        pvd = mm_group("b0", pan, [(pan[:, k * 128:(k + 1) * 128],
                                 h_bf[:, k * TC:(k + 1) * TC]) for k in range(KT_D)],
                       waits0=[wt, ("V", hbf_V)])
        wsl_free[s] = ("P", pvd)
        s_op(lambda e: e.copy(dtr_bf[:, :], pb[0][:, :]), waits=[("P", pvd)])
        setbank("b0", ("Sc", S.c("Sc")))
        dtr_Sc = S.c("Sc")

        grpV = getattr(build_nc, "_grpV", None)
        if not hasattr(S, "mixg"):
            S.mixg = {"grpV": [0, 0], "grpP_C": [0, 0],
                      "pairP_B": [0, 0], "pairP_xb": [0, 0]}
        MG = S.mixg
        if "hsG" not in MG:
            MG["hsG"] = [0, 0]

        def c_bf(b, j):
            c0 = b * 512 + j * 256
            return poolF4[:, c0:c0 + 256].bitcast(BF16)

        def B_bfv(pb_, h=None):
            c0 = 1024 + pb_ * 256
            if h is None:
                return poolF4[:, c0:c0 + 256].bitcast(BF16)
            return poolF4[h * 64:h * 64 + 64, c0:c0 + 256].bitcast(BF16)

        def xb_bfv(pb_, h=None):
            c0 = 1536 + pb_ * 256
            if h is None:
                return poolF4[:, c0:c0 + 256].bitcast(BF16)
            return poolF4[h * 64:h * 64 + 64, c0:c0 + 256].bitcast(BF16)

        def zsil_g(b, j):
            return poolF1[:, (0 + b) * 1024 + j * TC:(0 + b) * 1024 + (j + 1) * TC]

        def dt_g(b, j):
            return poolF1[:, (2 + b) * 1024 + j * TC:(2 + b) * 1024 + (j + 1) * TC]

        def dtx_g(b, j):
            return poolF1[:, (4 + b) * 1024 + j * TC:(4 + b) * 1024 + (j + 1) * TC]

        def xsil_g(b, j):
            return poolF1[:, (6 + b) * 1024 + j * TC:(6 + b) * 1024 + (j + 1) * TC]

        def xcb_bj(b, j, a, z2):
            base = 3072 + (b * 2 + j) * 520
            return poolF2[:, base + a:base + z2]

        def ktiles(pan):
            return [(pan[:, k * 128:(k + 1) * 128],
                     h_bf[:, k * TC:(k + 1) * TC]) for k in range(KT_D)]

        def emit_inproj_group(g):
            b = g % 2
            p_ = g // 2
            pb_ = p_ % 2
            if g % 2 == 0:
                s, pan, wt = wstream(s_ipw, MT_XB0 + p_, 2048)
                pv = mm_group("b0", pan, ktiles(pan), waits0=[wt])
                wsl_free[s] = ("P", pv)
                v_op(lambda e, pb_=pb_: e.tensor_scalar_mul(xb_bfv(pb_), pb[0][:, :], 1.0),
                     waits=[("P", pv), ("P", MG["pairP_xb"][pb_])])
                setbank("b0", ("V", S.c("V")))
                s, pan, wt = wstream(s_ipw, MT_B0 + p_, 2048)
                pv = mm_group("b1", pan, ktiles(pan), waits0=[wt])
                wsl_free[s] = ("P", pv)
                v_op(lambda e, pb_=pb_: e.tensor_scalar_mul(B_bfv(pb_), pb[1][:, :], 1.0),
                     waits=[("P", pv), ("P", MG["pairP_B"][pb_])])
                setbank("b1", ("V", S.c("V")))
            xbB_V = S.c("V")
            for j in range(2):
                ci = (g % 2) * 2 + j
                ct = 2 * g + j
                bk = j % 2
                t_mm(pb[bk][:, :],
                     R4_sb[(ci // 2) * 64:(ci // 2) * 64 + 64,
                           (ci % 2) * 128:(ci % 2 + 1) * 128],
                     xb_bfv(pb_, ci // 2), True, True,
                     waits=[("V", xbB_V)] + wbank("b%d" % bk), inc="P")
                pv = S.c("P")
                s_op(lambda e, b=b, j=j, bk=bk: e.copy(xcb_bj(b, j, 3, 515), pb[bk][:, :]),
                     waits=[("P", pv), ("V", MG["grpV"][b])])
                setbank("b%d" % bk, ("Sc", S.c("Sc")))
                scv = S.c("Sc")
                v_op(lambda e, b=b, j=j, ct=ct: e.tensor_copy(
                    xcb_bj(b, j, 0, 3), tails[:, ct * 3:(ct + 1) * 3]), waits=[("Sc", scv)])
                v_op(lambda e, b=b, j=j, ct=ct: e.tensor_scalar(
                    dtx_g(b, j), xcb_bj(b, j, 0, 512), convw_sb[:, ct * 4:ct * 4 + 1],
                    None, op0=ALU.mult), waits=[("V", S.c("V"))])
                for tap in range(1, 4):
                    v_op(lambda e, b=b, j=j, ct=ct, tap=tap: e.scalar_tensor_tensor(
                        dtx_g(b, j), xcb_bj(b, j, tap, tap + 512),
                        convw_sb[:, ct * 4 + tap:ct * 4 + tap + 1], dtx_g(b, j),
                        op0=ALU.mult, op1=ALU.add))
                v_op(lambda e, b=b, j=j, ct=ct: e.tensor_copy(
                    tails[:, ct * 3:(ct + 1) * 3], xcb_bj(b, j, 512, 515)))
            conv_V = S.c("V")
            for j in range(2):
                s, pan, wt = wstream(s_ipw, MT_C0 + 2 * g + j, 2048)
                bk = j % 2
                pv = mm_group("b%d" % bk, pan, ktiles(pan), waits0=[wt])
                wsl_free[s] = ("P", pv)
                v_op(lambda e, b=b, j=j, bk=bk: e.tensor_scalar_mul(
                    c_bf(b, j), pb[bk][:, :], 1.0),
                    waits=[("P", pv), ("P", MG["grpP_C"][b])])
                setbank("b%d" % bk, ("V", S.c("V")))
            C_V = S.c("V")
            s = wdma_i[0] % NWSL
            wdma_i[0] += 1
            for j in range(2):
                dma("sync", wpan[:, s * 2048 + j * 128: s * 2048 + (j + 1) * 128],
                    s_dtw[2 * g + j, :, :], "W", waits=[wsl_free[s], dtw_ready])
            wv = S.c("W")
            for j in range(2):
                ct = 2 * g + j
                bk = j % 2
                t_mm(pb[bk][:, :], wpan[:, s * 2048 + j * 128: s * 2048 + (j + 1) * 128],
                     dtr_bf[:, :], True, True,
                     waits=[("W", wv), ("Sc", dtr_Sc)] + wbank("b%d" % bk), inc="P")
                pv = S.c("P")
                s_op(lambda e, j=j, ct=ct, bk=bk: e.activation(
                    sp_a(0)[:, j * TC:(j + 1) * TC] if False else poolF2[:, j * TC:(j + 1) * TC],
                    pb[bk][:, :], AF.Exp, bias=dtb_sb[:, ct:ct + 1]),
                    waits=[("P", pv)])
                s_op(lambda e, b=b, j=j: e.activation(
                    dt_g(b, j), poolF2[:, j * TC:(j + 1) * TC], AF.Ln, bias=1.0),
                    waits=[("V", MG["grpV"][b])])
                setbank("b%d" % bk, ("Sc", S.c("Sc")))
            wsl_free[s] = ("P", S.c("P"))
            zpv = []
            for j in range(2):
                s, pan, wt = wstream(s_ipw, MT_Z0 + 2 * g + j, 2048)
                bk = j % 2
                pv = mm_group("b%d" % bk, pan, ktiles(pan), waits0=[wt])
                wsl_free[s] = ("P", pv)
                s_op(lambda e, b=b, j=j, bk=bk: e.activation(
                    zsil_g(b, j), pb[bk][:, :], AF.Silu),
                    waits=[("P", pv), ("V", MG["grpV"][b])])
                setbank("b%d" % bk, ("Sc", S.c("Sc")))
            for j in range(2):
                ct = 2 * g + j
                s_op(lambda e, b=b, j=j, ct=ct: e.activation(
                    xsil_g(b, j), dtx_g(b, j), AF.Silu, bias=convb_sb[:, ct:ct + 1]),
                    waits=[("V", conv_V)])
            xsil_Sc = S.c("Sc")
            for j in range(2):
                v_op(lambda e, b=b, j=j: e.tensor_tensor(
                    dtx_g(b, j), dt_g(b, j), xsil_g(b, j), op=ALU.mult),
                    waits=[("Sc", xsil_Sc)])
            return S.c("V"), S.c("Sc")

        def emit_scan_half(g, j, ready_V, ready_Sc):
            b = g % 2
            p_ = g // 2
            pb_ = p_ % 2
            ci = (g % 2) * 2 + j
            ct = 2 * g + j
            ysum_bank = "st" if ct % 2 == 0 else "bc"
            pend = None
            for n0 in range(0, DSTATE, 2):
                for u in range(2):
                    n = n0 + u
                    vb = (ci % 2) * 16 + n
                    t_mm(pbB[:, u * TC:(u + 1) * TC],
                         RB_sb[(ci // 2) * 64:(ci // 2) * 64 + 64,
                               vb * 128:(vb + 1) * 128],
                         B_bfv(pb_, ci // 2), True, True,
                         waits=([("V", ready_V)] + wbank("BB") if u == 0 else []),
                         inc="P" if u == 1 else None)
                bbc_P = S.c("P")
                dsl = []
                for u in range(2):
                    n = n0 + u
                    sl = (n0 + u) % 3
                    s_op(lambda e, b=b, j=j, ct=ct, n=n, sl=sl: e.activation(
                        dA_b(sl), dt_g(b, j), AF.Exp,
                        scale=A_sb[:, ct * 16 + n:ct * 16 + n + 1]),
                        waits=[dA_free[sl], ("Sc", ready_Sc)])
                    dsl.append((sl, S.c("Sc")))
                dtx_base = (4 + b) * 1024 + j * TC
                dtx_bc = bass.AP(poolF1, dtx_base, [[8192, 128], [0, 2], [1, TC]])
                v_op(lambda e, dtx_bc=dtx_bc: e.tensor_tensor(
                    dbx2.rearrange("p (a f) -> p a f", f=TC), dtx_bc,
                    pbB[:, :].rearrange("p (a f) -> p a f", f=TC), op=ALU.mult),
                    waits=[("P", bbc_P), ("V", ready_V)])
                setbank("BB", ("V", S.c("V")))
                for u in range(2):
                    n = n0 + u
                    sl, dasc = dsl[u]
                    col = ct * 16 + n
                    v_op(lambda e, sl=sl, col=col, u=u: e.tensor_tensor_scan(
                        hs2[:, u * TC:(u + 1) * TC], dA_b(sl),
                        dbx2[:, u * TC:(u + 1) * TC], carry[:, col:col + 1],
                        op0=ALU.mult, op1=ALU.add),
                        waits=[("Sc", dasc), ("Gc", MG.get("carryG", {}).get(col, 0)),
                               ("Gc", MG["hsG"][u])])
                    dA_free[sl] = ("V", S.c("V"))
                    scanv = S.c("V")
                    S.emit("gpsimd", lambda e, col=col, u=u: e.tensor_copy(
                        carry[:, col:col + 1], hs2[:, u * TC + 511:u * TC + 512]),
                        waits=[("V", scanv)], inc="Gc")
                    MG.setdefault("carryG", {})[col] = S.c("Gc")
                    MG["hsG"][u] = S.c("Gc")
                for u in range(2):
                    n = n0 + u
                    t_mm(pbC[:, u * TC:(u + 1) * TC],
                         RC_sb[:, n * 128:(n + 1) * 128], c_bf(b, j), True, True,
                         waits=([("V", ready_V)] + wbank("CC") if u == 0 else []),
                         inc="P" if u == 1 else None)
                cbc_P = S.c("P")
                if pend is not None:
                    p_n0, p_yc = pend
                    for u in range(2):
                        n = p_n0 + u
                        t_mm(bank_ap[ysum_bank], Ib_sb[:, :],
                             yc_t[:, u * TC:(u + 1) * TC],
                             start=(n == 0), stop=False,
                             waits=[("V", p_yc)] + (wbank(ysum_bank) if n == 0 else []),
                             inc="P" if u == 1 else None)
                    yc_free[0] = ("P", S.c("P"))
                v_op(lambda e: e.tensor_tensor(yc2, hs2, pbC[:, :], op=ALU.mult),
                     waits=[("P", cbc_P), yc_free[0]])
                setbank("CC", ("V", S.c("V")))
                yc_V = S.c("V")
                pend = (n0, yc_V)
            p_n0, p_yc = pend
            for u in range(2):
                n = p_n0 + u
                t_mm(bank_ap[ysum_bank], Ib_sb[:, :],
                     yc_t[:, u * TC:(u + 1) * TC],
                     start=False, stop=(n == DSTATE - 1),
                     waits=[("V", p_yc)],
                     inc="P" if u == 1 else None)
            yc_free[0] = ("P", S.c("P"))
            MG["grpP_C"][b] = S.c("P")
            if g % 2 == 1:
                MG["pairP_B"][pb_] = S.c("P")
                MG["pairP_xb"][pb_] = S.c("P")
            ysum_P = S.c("P")
            v_op(lambda e, b=b, j=j, ct=ct, yb=ysum_bank: e.scalar_tensor_tensor(
                ygf, xsil_g(b, j), dpar_sb[:, ct:ct + 1], bank_ap[yb],
                op0=ALU.mult, op1=ALU.add),
                waits=[("P", ysum_P)])
            v_op(lambda e, b=b, j=j, ct=ct: e.tensor_tensor(
                ygt(ct), ygf, zsil_g(b, j), op=ALU.mult),
                waits=[("W", quant_done_W)])
            setbank(ysum_bank, ("V", S.c("V")))

        ready = emit_inproj_group(0)
        readies = {0: ready}
        for g in range(16):
            rv, rs = readies[g]
            emit_scan_half(g, 0, rv, rs)
            if g + 1 < 16:
                readies[g + 1] = emit_inproj_group(g + 1)
            emit_scan_half(g, 1, rv, rs)
            MG["grpV"][g % 2] = S.c("V")

        yg_V = S.c("V")
        sbar("B")
        if kc == 0:
            dbg_dump("dt")
            dbg_dump("xsil")
            dbg_dump("zsil")
            dbg_dump("ygated")
            dbg_dump("carry")

        opw_ready = wready["s_opw"]
        for m in range(MT_OP):
            s1_, pan1, wt1 = wstream(s_opw, m, 2048, waits_extra=[opw_ready])
            s2_, pan2, wt2 = wstream(s_opw, m, 2048, col0=2048)
            bk = m % 2
            tiles = [(pan1[:, k * 128:(k + 1) * 128], ygt(k)) for k in range(16)] + \
                    [(pan2[:, k * 128:(k + 1) * 128], ygt(16 + k)) for k in range(16)]
            pv = mm_group("b%d" % bk, None, tiles, waits0=[wt2, ("V", yg_V)])
            wsl_free[s1_] = ("P", pv)
            wsl_free[s2_] = ("P", pv)
            v_op(lambda e, m=m, bk=bk: e.tensor_tensor(
                hres(m), pb[bk][:, :], x_fm[:, m * TC:(m + 1) * TC], op=ALU.add),
                waits=[("P", pv)])
            v_op(lambda e, m=m: e.tensor_tensor(
                t1pre(m), hres(m), x_fm[:, m * TC:(m + 1) * TC], op=ALU.add))
            setbank("b%d" % bk, ("V", S.c("V")))
        hres_V = S.c("V")
        sbar("D")
        if kc == 0:
            dbg_dump("hres")

        pv_bc = ln_stats_rms([hres(m) for m in range(KT_D)], EPS_RMS, 1.0 / D_MODEL, hres_V)
        for m in range(KT_D):
            v_op(lambda e, m=m: e.scalar_tensor_tensor(
                h_bf[:, m * TC:(m + 1) * TC], hres(m),
                ln2_sb[:, m:m + 1], pbc[:, :], op0=ALU.mult, op1=ALU.mult),
                waits=[("P", pv_bc)])
        setbank("bc", ("V", S.c("V")))
        h2_V = S.c("V")
        h2_Sc = S.c("Sc")
        sbar("rms2")
        if kc == 0:
            dbg_dump("hbf2")

        gw_ready = wready["s_gw"]
        uw_ready = wready["s_uw"]
        dw_ready = wready["s_dw"]

        def mlp_ffn(rhs_bf, rhs_ready, scr_g, scr_u, w_gate, scr_dn, n_m,
                    two_tensors, ffn_stats, acc_add_always, acc_gate_Sc):
            for m in range(n_m):
                s, pan, wt = wstream(scr_g, m, 2048, waits_extra=[w_gate])
                bkg = "b0" if (m % 2 == 0) else "B0"
                pvg = mm_group(bkg, pan,
                               [(pan[:, k * 128:(k + 1) * 128],
                                 rhs_bf[:, k * TC:(k + 1) * TC]) for k in range(KT_D)],
                               waits0=[wt, rhs_ready])
                wsl_free[s] = ("P", pvg)
                s2, pan2, wt2 = wstream(scr_u, m + (64 if two_tensors else 0), 2048,
                                        waits_extra=[w_gate], q="gpsimd", sem="U")
                bku = "b1" if (m % 2 == 0) else "B1"
                pvu = mm_group(bku, pan2,
                               [(pan2[:, k * 128:(k + 1) * 128],
                                 rhs_bf[:, k * TC:(k + 1) * TC]) for k in range(KT_D)],
                               waits0=[wt2, rhs_ready])
                wsl_free[s2] = ("P", pvu)
                gsl = m % 2
                s_op(lambda e, gsl=gsl, bkg=bkg: e.activation(
                    gs_b(gsl), bank_ap[bkg], AF.Silu),
                    waits=[("P", pvg), gs_free[gsl]])
                setbank(bkg, ("Sc", S.c("Sc")))
                scg = S.c("Sc")
                hsl = m % 16
                v_op(lambda e, gsl=gsl, bku=bku, hsl=hsl: e.tensor_tensor(
                    hring(hsl), gs_b(gsl), bank_ap[bku], op=ALU.mult),
                    waits=[("Sc", scg), ("P", pvu), hring_free[hsl]])
                setbank(bku, ("V", S.c("V")))
                gs_free[gsl] = ("V", S.c("V"))
                hr_V = S.c("V")
                if ffn_stats:
                    ssl = m % 2
                    s_op(lambda e, ssl=ssl, hsl=hsl: e.activation(
                        sq_b(ssl), hring(hsl), AF.Square),
                        waits=[("V", hr_V), sq_free[ssl]])
                    scv = S.c("Sc")
                    t_mm(pst[0:1, :], onesb_sb[:, 0:1], sq_b(ssl),
                         start=(m == 0), stop=(m == n_m - 1),
                         waits=[("Sc", scv)] + (wbank("st") if m == 0 else []),
                         inc="P")
                    sq_free[ssl] = ("P", S.c("P"))
                if m % 16 == 15:
                    b = m // 16
                    for mo in range(MT_OP):
                        so = wdma_i[0] % NWSL
                        wdma_i[0] += 1
                        dstw = wpan[:, so * 2048:(so + 1) * 2048]
                        dma("sync", dstw, scr_dn[mo, :, b * 2048:(b + 1) * 2048], "W",
                            waits=[wsl_free[so], w_gate])
                        wvd = S.c("W")
                        bkd = "C0" if mo % 2 == 0 else "C1"
                        vneed = S.c("V")
                        for k in range(16):
                            t_mm(bank_ap[bkd], dstw[:, k * 128:(k + 1) * 128],
                                 hring(k), start=(k == 0), stop=(k == 15),
                                 waits=([("W", wvd), ("V", vneed)] + wbank(bkd)
                                        if k == 0 else []),
                                 inc="P" if k == 15 else None)
                        pvd2 = S.c("P")
                        wsl_free[so] = ("P", pvd2)
                        if acc_add_always or b > 0:
                            v_op(lambda e, mo=mo, bkd=bkd: e.tensor_tensor(
                                t1pre(mo) if acc_add_always else hres(mo),
                                t1pre(mo) if acc_add_always else hres(mo),
                                bank_ap[bkd], op=ALU.add),
                                waits=[("P", pvd2)])
                        else:
                            v_op(lambda e, mo=mo, bkd=bkd: e.tensor_copy(
                                hres(mo), bank_ap[bkd]),
                                waits=[("P", pvd2), ("Sc", acc_gate_Sc)])
                        setbank(bkd, ("V", S.c("V")))
                    for k in range(16):
                        hring_free[k] = ("P", S.c("P"))

        mlp_ffn(h_bf, ("V", h2_V), s_gw, s_uw,
                ("G", max(gw_ready[1], uw_ready[1], dw_ready[1])),
                s_dw, MT_G, False, False, True, 0)
        mlp_V = S.c("V")
        sbar("E")
        if kc == 0:
            dbg_dump("t1pre")

        def full_ln(buf_f, out_bf, wait_V):
            for m in range(KT_D):
                s_op(lambda e, m=m: e.copy(h_bf[:, m * TC:(m + 1) * TC], buf_f(m)),
                     waits=[("V", wait_V)])
                scv = S.c("Sc")
                t_mm(pst[0:1, :], onesb_sb[:, 0:1], h_bf[:, m * TC:(m + 1) * TC],
                     start=(m == 0), stop=(m == KT_D - 1),
                     waits=[("Sc", scv)] + (wbank("st") if m == 0 else []),
                     inc="P")
            pv = S.c("P")
            v_op(lambda e: e.tensor_scalar_mul(statC_r, pst[0:1, :],
                                               1.0 / D_MODEL), waits=[("P", pv)])
            setbank("st", ("V", S.c("V")))
            for m in range(KT_D):
                ssl = m % 2
                s_op(lambda e, m=m, ssl=ssl: e.activation(sq_b(ssl), buf_f(m), AF.Square),
                     waits=[sq_free[ssl], ("V", wait_V)])
                scv = S.c("Sc")
                t_mm(pst[0:1, :], onesb_sb[:, 0:1], sq_b(ssl),
                     start=(m == 0), stop=(m == KT_D - 1),
                     waits=[("Sc", scv)] + (wbank("st") if m == 0 else []),
                     inc="P")
                sq_free[ssl] = ("P", S.c("P"))
            pv = S.c("P")
            v_op(lambda e: e.tensor_scalar_mul(stats[0:1, 0:512], pst[0:1, :],
                                               1.0 / D_MODEL), waits=[("P", pv)])
            setbank("st", ("V", S.c("V")))
            v_op(lambda e: e.tensor_tensor(stats[0:1, 512:1024], statC_f,
                                           statC_f, op=ALU.mult))
            v_op(lambda e: e.tensor_tensor(stats[0:1, 512:1024], stats[0:1, 0:512],
                                           stats[0:1, 512:1024], op=ALU.subtract))
            vv = S.c("V")
            s_op(lambda e: e.activation(stats[0:1, 0:512], stats[0:1, 512:1024],
                                        AF.Ln, bias=EPS_LN), waits=[("V", vv)])
            s_op(lambda e: e.activation(stats[0:1, 512:1024], stats[0:1, 0:512],
                                        AF.Exp, scale=-0.5))
            scv = S.c("Sc")
            v_op(lambda e: e.tensor_tensor(statC_r, statC_f,
                                           stats[0:1, 512:1024], op=ALU.mult),
                 waits=[("Sc", scv)])
            v_op(lambda e: e.tensor_scalar_mul(statr[0:1, 0:512],
                                               stats[0:1, 512:1024], 1.0))
            vv = S.c("V")
            t_mm(pbc[:, :], ones1_sb[:, :], statr[0:1, 0:512], True, True,
                 waits=[("V", vv)] + wbank("bc"), inc="P")
            pv = S.c("P")
            for m in range(KT_D):
                v_op(lambda e, m=m: e.tensor_tensor(buf_f(m), buf_f(m), pbc[:, :],
                                                    op=ALU.mult), waits=[("P", pv)])
            setbank("bc", ("V", S.c("V")))
            vv = S.c("V")
            t_mm(pbc[:, :], ones1_sb[:, :], statr[0:1, 512:1024], True, True,
                 waits=[("V", vv)] + wbank("bc"), inc="P")
            pv = S.c("P")
            for m in range(KT_D):
                v_op(lambda e, m=m: e.tensor_tensor(buf_f(m), buf_f(m), pbc[:, :],
                                                    op=ALU.subtract), waits=[("P", pv)])
            setbank("bc", ("V", S.c("V")))
            if out_bf is not None:
                vfin = S.c("V")
                for m in range(KT_D):
                    s_op(lambda e, m=m: e.copy(out_bf[:, m * TC:(m + 1) * TC], buf_f(m)),
                         waits=[("V", vfin)])

        full_ln(t1pre, h_bf, mlp_V)
        x1_Sc = S.c("Sc")
        sbar("LN")
        if kc == 0:
            dbg_dump("x1bf")
            dbg_dump("lnstats")
            dbg_dump("lnstatr")
            dbg_dump("x1f")

        mlp_ffn(h_bf, ("Sc", x1_Sc), s_qgu, s_qgu, ("W", quant_done_W),
                s_qp, MT_G, True, True, False, h2_Sc)
        sbar("F")
        if kc == 0:
            dbg_dump("t2pre")
        pv = S.c("P")
        s_op(lambda e: e.activation(stats[0:1, 0:512], pst[0:1, :], AF.Ln,
                                    bias=EPS_F32, scale=1.0 / 8192), waits=[("P", pv)])
        s_op(lambda e: e.activation(stats[0:1, 512:1024], stats[0:1, 0:512], AF.Exp,
                                    scale=-0.5))
        setbank("st", ("Sc", S.c("Sc")))
        scv = S.c("Sc")
        v_op(lambda e: e.tensor_scalar_mul(statr[0:1, 0:512], stats[0:1, 512:1024], 1.0),
             waits=[("Sc", scv)])
        vv = S.c("V")
        t_mm(pbc[:, :], ones1_sb[:, :], statr[0:1, 0:512], True, True,
             waits=[("V", vv)] + wbank("bc"), inc="P")
        pv = S.c("P")
        for m in range(KT_D):
            v_op(lambda e, m=m: e.tensor_tensor(hres(m), hres(m), pbc[:, :],
                                                op=ALU.mult), waits=[("P", pv)])
            v_op(lambda e, m=m: e.tensor_tensor(hres(m), hres(m), t1pre(m), op=ALU.add))
        setbank("bc", ("V", S.c("V")))
        sbar("t2")
        if kc == 0:
            dbg_dump("t2")

        full_ln(hres, None, S.c("V"))
        out_V = S.c("V")

        for tp in range(4):
            osl = tp % 2
            for dg in range(4):
                bk = dg % 2
                for j in range(4):
                    dt16 = dg * 4 + j
                    t_tr(pb[bk][:, j * 128:(j + 1) * 128],
                         hres(dt16)[:, tp * 128:(tp + 1) * 128], If_sb[:, :],
                         waits=([("V", out_V), outst_free[osl]] + wbank("b%d" % bk)
                                if j == 0 else []),
                         inc="P" if j == 3 else None)
                pv = S.c("P")
                v_op(lambda e, osl=osl, dg=dg, bk=bk: e.tensor_copy(
                    outst(osl)[:, dg * TC:(dg + 1) * TC], pb[bk][:, :]),
                    waits=[("P", pv)])
                setbank("b%d" % bk, ("V", S.c("V")))
            vv = S.c("V")
            r0 = kc * TC + tp * 128
            dma("sync", d_out[r0:r0 + 128, :], outst(osl), "W", waits=[("V", vv)])
            outst_free[osl] = ("W", S.c("W"))
        x_load_ok = [("W", S.c("W")), ("P", S.c("P"))]

    S.emit("sync", lambda e: e.nop(), waits=[("W", S.c("W"))])

    @block.sync
    def _(sync):
        S.run("sync", sync, sems)

    @block.gpsimd
    def _(g):
        S.run("gpsimd", g, sems)

    @block.vector
    def _(v):
        S.run("vector", v, sems)

    @block.scalar
    def _(s):
        S.run("scalar", s, sems)

    @block.tensor
    def _(t):
        S.run("tensor", t, sems)

    stack.close()
    return nc


def _tile_w(w, nM, nK):
    a = np.ascontiguousarray(w).reshape(nM, 128, nK, 128)
    a = np.ascontiguousarray(a.transpose(0, 3, 2, 1))
    return np.ascontiguousarray(a.reshape(nM, 128, nK * 128))


def _make_consts():
    RB = np.zeros((32, 64, 128), np.float32)
    for q2 in range(2):
        for n in range(16):
            vb = q2 * 16 + n
            for pp in range(128):
                RB[vb, 32 * q2 + 16 * (pp // 64) + n, pp] = 1.0
    RC = np.zeros((16, 128, 128), np.float32)
    for n in range(16):
        for pp in range(128):
            RC[n, (pp // 16) * 16 + n, pp] = 1.0
    R4 = np.zeros((2, 64, 128), np.float32)
    for v in range(2):
        for pp in range(128):
            R4[v, 32 * v + 16 * (pp // 64) + (pp % 16), pp] = 1.0
    I = np.eye(128, dtype=np.float32)
    Ib = np.eye(128, dtype=np.float32).astype(ml_dtypes.bfloat16)
    onesb = np.ones((128, 1), np.float32).astype(ml_dtypes.bfloat16)
    ones1 = np.ones((1, 128), np.float32)
    return RB, RC, R4, I, Ib, onesb, ones1


def kernel(L0_attn_hidden_states, L0_attn_in_proj_w, L0_attn_conv_w,
           L0_attn_conv_b, L0_attn_dt_proj_w, L0_attn_dt_proj_b, L0_attn_A_log,
           L0_attn_D_param, L0_attn_out_proj_w, L0_attn_ln1_w, L0_attn_ln2_w,
           L0_attn_gate_w, L0_attn_up_w, L0_attn_down_w, L0_ffn_w_gate_up,
           L0_ffn_w_proj):
    x = np.asarray(L0_attn_hidden_states)
    RB, RC, R4, I, Ib, onesb, ones1 = _make_consts()
    common = {
        "ipw": _tile_w(np.asarray(L0_attn_in_proj_w), MT_PROJ, KT_D),
        "dtw": _tile_w(np.asarray(L0_attn_dt_proj_w), 32, 1).reshape(32, 128, 128),
        "opw": _tile_w(np.asarray(L0_attn_out_proj_w), MT_OP, KT_OP),
        "gw": _tile_w(np.asarray(L0_attn_gate_w), MT_G, KT_D),
        "uw": _tile_w(np.asarray(L0_attn_up_w), MT_G, KT_D),
        "dw": _tile_w(np.asarray(L0_attn_down_w), MT_OP, KT_DN),
        "qgu": np.ascontiguousarray(np.asarray(L0_ffn_w_gate_up)),
        "qp": np.ascontiguousarray(np.asarray(L0_ffn_w_proj)),
        "convw": np.ascontiguousarray(np.asarray(L0_attn_conv_w).reshape(D_INNER, 4)),
        "convb": np.asarray(L0_attn_conv_b),
        "dtb": np.asarray(L0_attn_dt_proj_b),
        "alog": np.asarray(L0_attn_A_log),
        "dpar": np.asarray(L0_attn_D_param),
        "ln1": np.asarray(L0_attn_ln1_w),
        "ln2": np.asarray(L0_attn_ln2_w),
        "RB": RB, "RC": RC, "R4": R4, "Imat": I, "Imatb": Ib,
        "onesb": onesb, "ones1": ones1,
        "fconst": np.broadcast_to(np.array(
            [EPS_RMS, 1.0, EPS_F32, 1.0 / D_MODEL, -0.5, 1.0 / 8192.0, 0.0, 0.0],
            np.float32), (128, 8)).copy(),
    }
    nc = build_nc()
    in_maps = []
    for b in range(8):
        m = dict(common)
        m["x"] = np.ascontiguousarray(x[b])
        in_maps.append(m)
    res = run_bass_kernel_spmd(nc, in_maps, core_ids=list(range(8)))
    out = np.stack([res.results[b]["out"] for b in range(8)], axis=0)
    return out.astype(np.float32)
